# version 04 (final)
# baseline (speedup 1.0000x reference)
import numpy as np
from contextlib import ExitStack
import concourse.bass as bass
import concourse.mybir as mybir

F32 = mybir.dt.float32
BF16 = mybir.dt.bfloat16
I32 = mybir.dt.int32
AF = mybir.ActivationFunctionType
ALU = mybir.AluOpType
AX = mybir.AxisListType

NCORES = 8
T = 2048
D = 1024
FI = 4384
DI = 2048
H = 32
PD = 64
CD = 2304
CH = 128
NCH = T // CH
TBS = 512
NTB = T // TBS
FFD = 4096
NLOC = 56
NCAND = NCORES * NLOC
PAY = 64 + PD * NLOC + 256
SHID = 256


class Sem:
    def __init__(self, nc, ctx, name):
        self.h = ctx.enter_context(nc.semaphore(name))
        self.n = 0

    def inc(self, instr, k=1):
        instr.then_inc(self.h, k)
        self.n += k
        return instr

    def dma(self, instr):
        return self.inc(instr, 16)


def build_nc(dbg=(), stage=99):
    nc = bass.Bass("TRN2", target_bir_lowering=False, debug=False, detect_race_conditions=False)
    dram = nc.dram_tensor

    x_f = dram("x_f", [D, T], F32, kind="ExternalInput").ap()
    normw = dram("normw", [D], F32, kind="ExternalInput").ap()
    inprojw = dram("inprojw", [D, FI], F32, kind="ExternalInput").ap()
    convw = dram("convw", [CD, 4], F32, kind="ExternalInput").ap()
    convb = dram("convb", [CD], F32, kind="ExternalInput").ap()
    dtbias = dram("dtbias", [H], F32, kind="ExternalInput").ap()
    alog = dram("alog", [H], F32, kind="ExternalInput").ap()
    dvec = dram("dvec", [H], F32, kind="ExternalInput").ap()
    ssmw = dram("ssmw", [DI], F32, kind="ExternalInput").ap()
    outpw = dram("outpw", [DI, D], F32, kind="ExternalInput").ap()
    sc1w = dram("sc1w", [D, SHID], F32, kind="ExternalInput").ap()
    sc2w = dram("sc2w", [SHID, 1], F32, kind="ExternalInput").ap()
    summw = dram("summw", [D, PD], F32, kind="ExternalInput").ap()
    qw = dram("qw", [D, PD], F32, kind="ExternalInput").ap()
    kw = dram("kw", [PD, PD], F32, kind="ExternalInput").ap()
    vw = dram("vw", [PD, D], F32, kind="ExternalInput").ap()
    gatew = dram("gatew", [2 * D, D], F32, kind="ExternalInput").ap()
    w1t = dram("w1t", [D, FFD], F32, kind="ExternalInput").ap()
    b1v = dram("b1v", [FFD], F32, kind="ExternalInput").ap()
    w2t = dram("w2t", [FFD, D], F32, kind="ExternalInput").ap()
    b2v = dram("b2v", [D], F32, kind="ExternalInput").ap()
    c_eye = dram("c_eye", [128, 128], F32, kind="ExternalInput").ap()
    c_tri = dram("c_tri", [128, 128], F32, kind="ExternalInput").ap()
    c_ones = dram("c_ones", [128, 1], F32, kind="ExternalInput").ap()
    c_onesr = dram("c_onesr", [1, 128], F32, kind="ExternalInput").ap()

    out_f = dram("out_f", [D, T], F32, kind="ExternalOutput").ap()
    dbg_out = {}
    for name, shape in dbg:
        dbg_out[name] = dram("dbg_" + name, list(shape), F32, kind="ExternalOutput").ap()

    d_xn = dram("d_xn", [D, T], F32).ap()
    d_xs = dram("d_xs", [DI, T], BF16).ap()
    d_z = dram("d_z", [DI, T], BF16).ap()
    d_h = dram("d_h", [FFD, T], F32).ap()
    d_su = dram("d_su", [D, T], F32).ap()
    d_s = dram("d_s", [D, T], F32).ap()
    d_wq1 = dram("d_wq1", [D, FFD], BF16).ap()
    d_wq2 = dram("d_wq2", [FFD, D], BF16).ap()
    d_sw1 = dram("d_sw1", [FFD], F32).ap()
    d_sw2 = dram("d_sw2", [D], F32).ap()
    d_cs = dram("d_cs", [H, T], F32).ap()
    d_row = dram("d_row", [4, T], F32).ap()
    d_esr = dram("d_esr", [NCH * H], F32).ap()
    t_ccin = dram("cc_in", [1, PAY], F32)
    t_ccout = dram("cc_out", [NCORES, PAY], F32)
    t_cc2in = dram("cc2_in", [1, 264], F32)
    t_cc2out = dram("cc2_out", [NCORES, 264], F32)
    cc_in, cc_out = t_ccin.ap(), t_ccout.ap()
    cc2_in, cc2_out = t_cc2in.ap(), t_cc2out.ap()

    ctx = ExitStack()
    sbA = lambda name, shape, dtp=F32: ctx.enter_context(nc.sbuf_tensor(name, shape, dtp))
    psal = lambda name, shape, dtp=F32: ctx.enter_context(nc.psum_tensor(name, shape, dtp))

    with ctx:
        t_eye32 = sbA("t_eye32", [128, 128], F32)
        t_eye16 = sbA("t_eye16", [128, 128], BF16)
        t_tri16 = sbA("t_tri16", [128, 128], BF16)
        t_ones32 = sbA("t_ones32", [128, 1], F32)
        t_ones16 = sbA("t_ones16", [128, 1], BF16)
        t_onesr16 = sbA("t_onesr16", [1, 128], BF16)
        t_normw = sbA("t_normw", [128, 8], F32)
        t_ssmw = sbA("t_ssmw", [128, 16], F32)
        t_convw = sbA("t_convw", [128, 18 * 4], F32)
        t_convb = sbA("t_convb", [128, 18], F32)
        t_dtb = sbA("t_dtb", [H, 1], F32)
        t_eA = sbA("t_eA", [H, 1], F32)
        t_alogc = sbA("t_alogc", [H, 1], F32)
        t_dpp = sbA("t_dpp", [128, 16], F32)
        t_sca = sbA("t_sca", [128, 8], F32)
        t_srow = sbA("t_srow", [1, 1024], F32)
        t_small = sbA("t_small", [1, 64], F32)
        t_r3b = sbA("t_r3b", [1, TBS], BF16)
        t_bc = sbA("t_bc", [128, 2 * T], BF16)
        t_tm = sbA("t_tm", [128, NCH * 96], F32)
        t_ecsc = sbA("t_ecsc", [128, 32], BF16)
        t_dall = sbA("t_dall", [128, NCH * H], BF16)
        t_esr = sbA("t_esr", [64, NCH * H], F32)
        t_summtm = sbA("t_summtm", [128, NCH * PD], BF16)
        t_qT = sbA("t_qT", [PD, T], BF16)
        t_scores = sbA("t_scores", [128, TBS], F32)
        t_sstat = sbA("t_sstat", [4, 2], F32)
        t_maxacc = sbA("t_maxacc", [128, 1], F32)
        t_maxcol = sbA("t_maxcol", [128, 1], F32)
        t_sw1 = sbA("t_sw1", [128, 32], F32)
        t_sw2 = sbA("t_sw2", [128, 8], F32)
        t_scl1 = sbA("t_scl1", [128, 32], F32)
        t_b1q = sbA("t_b1q", [128, 32], F32)
        t_scl2 = sbA("t_scl2", [128, 8], F32)
        t_b2q = sbA("t_b2q", [128, 8], F32)
        t_bcol = sbA("t_bcol", [128, 32], F32)
        t_bi32 = sbA("t_bi32", [128, 32], I32)
        t_scw = sbA("t_scw", [128, 8 * SHID], BF16)
        t_scw2 = sbA("t_scw2", [128, 2], BF16)
        t_sumw = sbA("t_sumw", [128, 8 * PD], BF16)
        t_qww = sbA("t_qww", [128, 8 * PD], BF16)
        t_kww = sbA("t_kww", [PD, PD], BF16)
        t_cand = sbA("t_cand", [4, NLOC], F32)
        t_g56 = sbA("t_g56", [1, NLOC], F32)
        t_sctm = sbA("t_sctm", [128, 16], F32)
        t_candrep = sbA("t_candrep", [128, NLOC], F32)
        t_selt = sbA("t_selt", [128, NLOC], BF16)
        t_pool = sbA("t_pool", [PD, NLOC], F32)
        t_vals = sbA("t_vals", [4, NCAND], F32)
        t_poolall = sbA("t_poolall", [PD, NCAND], BF16)
        t_kkT = sbA("t_kkT", [PD, NCAND], BF16)
        t_stat8 = sbA("t_stat8", [4, 64], F32)

        AR_A = sbA("AR_A", [128, 8 * T], BF16)
        AR_B = sbA("AR_B", [128, 8 * T], BF16)
        AR_C = sbA("AR_C", [128, 8 * T], F32)
        AR_D = sbA("AR_D", [128, 4 * T], F32)
        A_A = AR_A.ap()
        A_A32 = AR_A.ap().bitcast(F32)
        A_AI = AR_A.ap().bitcast(I32)
        A_B16 = AR_B.ap()
        A_B32 = AR_B.ap().bitcast(F32)
        A_C32 = AR_C.ap()
        A_C16 = AR_C.ap().bitcast(BF16)
        A_CI = AR_C.ap().bitcast(I32)
        A_D32 = AR_D.ap()
        A_D16 = AR_D.ap().bitcast(BF16)
        A_DI = AR_D.ap().bitcast(I32)
        t_hrows = A_D32[:, 2 * T:3 * T]
        t_scwork = A_C32[:, 0:TBS]
        t_cand1 = A_C32[:, TBS:TBS + 4 * NLOC]

        P = [psal(f"p{i}", [128, 512], F32) for i in range(8)]
        PB = [p.ap().bitcast(BF16) for p in P]

        sq = Sem(nc, ctx, "sq")
        gq = Sem(nc, ctx, "gq")
        sv = Sem(nc, ctx, "sv")
        ss = Sem(nc, ctx, "ss")
        stm = Sem(nc, ctx, "stm")
        sg = Sem(nc, ctx, "sg")
        scc = Sem(nc, ctx, "scc")
        sfin = Sem(nc, ctx, "sfin")
        szs = [Sem(nc, ctx, "szs0"), Sem(nc, ctx, "szs1")]
        sxs = [Sem(nc, ctx, "sxs0"), Sem(nc, ctx, "sxs1")]
        sxt = Sem(nc, ctx, "sxt")
        sop = Sem(nc, ctx, "sop")
        ssu = Sem(nc, ctx, "ssu")
        sr2 = Sem(nc, ctx, "sr2")
        srw = Sem(nc, ctx, "srw")
        ssl = Sem(nc, ctx, "ssl")
        ssm = Sem(nc, ctx, "ssm")
        ALLS = (sq, gq, sv, ss, stm, sg, scc, szs[0], szs[1], sxs[0], sxs[1], sxt, sop, ssu, sr2, srw, ssl, ssm)

        block = ctx.enter_context(nc.Block())

        def wait(e, *sems):
            for s in sems:
                if s.n:
                    e.wait_ge(s.h, s.n)

        def bcast_pat(ap2d, reps):
            return ap2d.unsqueeze(2).to_broadcast([ap2d.shape[0], ap2d.shape[1], reps])

        def _(s):
            sq.dma(s.dma_start(out=t_eye32[:], in_=c_eye[:, :]))
            sq.dma(s.dma_start(out=t_ones32[:], in_=c_ones[:, :]))
            sq.dma(s.dma_start(out=t_normw[:, 0:8], in_=normw.rearrange("(c p) -> p c", p=128), allow_slow_non_contiguous=True))
            sq.dma(s.dma_start(out=t_ssmw[:, 0:16], in_=ssmw.rearrange("(c p) -> p c", p=128), allow_slow_non_contiguous=True))
            sq.dma(s.dma_start(out=t_convw.ap().rearrange("p (c k) -> p c k", k=4), in_=convw.rearrange("(c p) k -> p c k", p=128), allow_slow_non_contiguous=True))
            sq.dma(s.dma_start(out=t_convb[:], in_=convb.rearrange("(c p) -> p c", p=128), allow_slow_non_contiguous=True))
            sq.dma(s.dma_start(out=t_dtb[:, 0:1], in_=dtbias.unsqueeze(1), allow_slow_non_contiguous=True))
            sq.dma(s.dma_start(out=t_alogc[:, 0:1], in_=alog.unsqueeze(1), allow_slow_non_contiguous=True))
            sq.dma(s.dma_start(out=t_dpp[0:64, :], in_=dvec.rearrange("(c two) -> two c", two=2)[0:1, :].partition_broadcast(64), allow_slow_non_contiguous=True))
            sq.dma(s.dma_start(out=t_dpp[64:128, :], in_=dvec.rearrange("(c two) -> two c", two=2)[1:2, :].partition_broadcast(64), allow_slow_non_contiguous=True))
        block.sync(_)

        def _(g):
            gq.dma(g.dma_start(out=t_eye16[:], in_=c_eye[:, :]))
            gq.dma(g.dma_start(out=t_tri16[:], in_=c_tri[:, :]))
            gq.dma(g.dma_start(out=t_ones16[:], in_=c_ones[:, :]))
            gq.dma(g.dma_start(out=t_onesr16[:], in_=c_onesr[:, :]))
            gq.dma(g.dma_start(out=t_scw.ap().rearrange("p (c f) -> p c f", f=SHID), in_=sc1w.rearrange("(c p) f -> p c f", p=128)))
            gq.dma(g.dma_start(out=t_scw2.ap().rearrange("p (c one) -> p c one", one=1), in_=sc2w.rearrange("(c p) one -> p c one", p=128), allow_slow_non_contiguous=True))
            gq.dma(g.dma_start(out=t_sumw.ap().rearrange("p (c f) -> p c f", f=PD), in_=summw.rearrange("(c p) f -> p c f", p=128), allow_slow_non_contiguous=True))
            gq.dma(g.dma_start(out=t_qww.ap().rearrange("p (c f) -> p c f", f=PD), in_=qw.rearrange("(c p) f -> p c f", p=128), allow_slow_non_contiguous=True))
            gq.dma(g.dma_start(out=t_kww[:], in_=kw[:, :]))
        block.gpsimd(_)

        def _(v):
            v.memset(t_small[:], 0.0)
            sv.inc(v.memset(t_sca[:], 0.0))
        block.vector(_)

        xall = lambda dc: A_C32[:, dc * T:(dc + 1) * T]
        xn2v = lambda dc: A_A[:, dc * T:(dc + 1) * T]
        sqb = lambda i: A_D16[:, (i % 2) * T:((i % 2) + 1) * T]
        SQ2 = A_D16[:, 4096:6144]
        SCR0 = A_D32[:, 0:T]
        SROW = t_srow.ap()[0:1, 0:512]
        SROWB = t_srow.ap()[0:1, 512:1024]
        REP_a = A_D32[:, 2 * T:3 * T]
        REP_b = A_D32[:, 3 * T:4 * T]

        sq_x0 = sq.n

        def _(s):
            for dc in range(8):
                sq.dma(s.dma_start(out=xall(dc), in_=x_f[dc * 128:(dc + 1) * 128, :]))
        block.sync(_)
        xload = [sq_x0 + 16 * (dc + 1) for dc in range(8)]

        def _(t):
            t.wait_ge(sq.h, xload[7])
            for dc in range(8):
                last = None
                for nb in range(4):
                    last = t.matmul(P[nb].ap()[0:1, :], t_ones32.ap()[:, :], xall(dc)[:, nb * 512:(nb + 1) * 512],
                                    start=(dc == 0), stop=(dc == 7))
                stm.inc(last)
        block.tensor(_)

        def _(v):
            wait(v, stm)
            for nb in range(4):
                if nb:
                    v.wait_ge(srw.h, 16 * nb)
                sv.inc(v.tensor_scalar(SROW, P[nb].ap()[0:1, :], 1.0 / D, None, ALU.mult))
        block.vector(_)
        sv_mrow = sv.n - 4

        def _(s):
            for nb in range(4):
                s.wait_ge(sv.h, sv_mrow + nb + 1)
                srw.dma(s.dma_start(out=d_row[0:1, nb * 512:(nb + 1) * 512], in_=SROW))
            s.wait_ge(srw.h, srw.n)
            sq.dma(s.dma_start(out=REP_a, in_=d_row[0:1, :].partition_broadcast(128)))
        block.sync(_)
        rep_done = sq.n

        sv_cen0 = sv.n

        def _(v):
            v.wait_ge(sq.h, rep_done)
            for dc in range(8):
                sv.inc(v.tensor_tensor(xall(dc), xall(dc), REP_a, ALU.subtract))
        block.vector(_)

        ss_sq0 = ss.n
        st_stat0 = stm.n

        def _(sc):
            for dc in range(8):
                sc.wait_ge(sv.h, sv_cen0 + dc + 1)
                if dc >= 2:
                    sc.wait_ge(stm.h, st_stat0 + (dc - 1))
                ss.inc(sc.activation(out=sqb(dc), in_=xall(dc), func=AF.Square, bias=0.0, scale=1.0))
        block.scalar(_)

        def _(t):
            wait(t, gq)
            for dc in range(8):
                t.wait_ge(ss.h, ss_sq0 + dc + 1)
                last = None
                for nb in range(4):
                    last = t.matmul(P[4 + nb].ap()[0:1, :], t_ones16.ap()[:, :], sqb(dc)[:, nb * 512:(nb + 1) * 512],
                                    start=(dc == 0), stop=(dc == 7))
                stm.inc(last)
        block.tensor(_)

        sv_vr0 = sv.n
        ss_vr0 = ss.n
        srw_vr0 = srw.n

        def _(v):
            wait(v, stm)
            for nb in range(4):
                if nb:
                    v.wait_ge(srw.h, srw_vr0 + 16 * nb)
                sv.inc(v.tensor_scalar(SROWB, P[4 + nb].ap()[0:1, :], 1.0 / D, 1e-5, ALU.mult, ALU.add))
                v.wait_ge(sv.h, sv.n)
                sv.inc(v.reciprocal(SROW, SROWB))
        block.vector(_)

        def _(sc):
            for nb in range(4):
                sc.wait_ge(sv.h, sv_vr0 + 2 * (nb + 1))
                ss.inc(sc.activation(out=SROW, in_=SROW, func=AF.Sqrt, bias=0.0, scale=1.0))
        block.scalar(_)

        def _(s):
            for nb in range(4):
                s.wait_ge(ss.h, ss_vr0 + nb + 1)
                srw.dma(s.dma_start(out=d_row[1:2, nb * 512:(nb + 1) * 512], in_=SROW))
            s.wait_ge(srw.h, srw.n)
            sq.dma(s.dma_start(out=REP_b, in_=d_row[1:2, :].partition_broadcast(128)))
        block.sync(_)
        rep_done2 = sq.n

        sv_xn0 = sv.n

        def _(v):
            v.wait_ge(sq.h, rep_done2)
            for dc in range(8):
                v.tensor_tensor(xall(dc), xall(dc), REP_b, ALU.mult)
                sv.inc(v.reduce_max(t_maxcol[:], xall(dc), axis=AX.X))
                v.wait_ge(sv.h, sv.n)
                if dc == 0:
                    sv.inc(v.tensor_copy(t_maxacc[:], t_maxcol[:]))
                else:
                    sv.inc(v.tensor_tensor(t_maxacc[:], t_maxacc[:], t_maxcol[:], ALU.max))
        block.vector(_)

        def _(s):
            for dc in range(8):
                s.wait_ge(sv.h, sv_xn0 + 2 * dc + 1)
                sq.dma(s.dma_start(out=d_xn[dc * 128:(dc + 1) * 128, :], in_=xall(dc)))
        block.sync(_)

        ss_s20 = ss.n

        def _(sc):
            for dc in range(8):
                sc.wait_ge(sv.h, sv_xn0 + 2 * dc + 1)
                if dc >= 1:
                    sc.wait_ge(sr2.h, dc)
                ss.inc(sc.activation(out=SQ2, in_=xall(dc), func=AF.Square, bias=0.0, scale=1.0))
        block.scalar(_)

        def _(t):
            for dc in range(8):
                t.wait_ge(ss.h, ss_s20 + dc + 1)
                last = None
                for nb in range(4):
                    last = t.matmul(P[nb].ap()[0:1, :], t_ones16.ap()[:, :], SQ2[:, nb * 512:(nb + 1) * 512],
                                    start=(dc == 0), stop=(dc == 7))
                sr2.inc(last)
        block.tensor(_)

        sv_r20 = sv.n
        ss_r20 = ss.n
        srw_r20 = srw.n

        def _(v):
            wait(v, sr2)
            for nb in range(4):
                if nb:
                    v.wait_ge(srw.h, srw_r20 + 16 * nb)
                sv.inc(v.tensor_scalar(SROWB, P[nb].ap()[0:1, :], 1.0 / D, 1e-5, ALU.mult, ALU.add))
                v.wait_ge(sv.h, sv.n)
                sv.inc(v.reciprocal(SROW, SROWB))
        block.vector(_)

        def _(sc):
            for nb in range(4):
                sc.wait_ge(sv.h, sv_r20 + 2 * (nb + 1))
                ss.inc(sc.activation(out=SROW, in_=SROW, func=AF.Sqrt, bias=0.0, scale=1.0))
        block.scalar(_)

        def _(s):
            for nb in range(4):
                s.wait_ge(ss.h, ss_r20 + nb + 1)
                srw.dma(s.dma_start(out=d_row[2:3, nb * 512:(nb + 1) * 512], in_=SROW))
            s.wait_ge(srw.h, srw.n)
            sq.dma(s.dma_start(out=REP_a, in_=d_row[2:3, :].partition_broadcast(128)))
        block.sync(_)
        rep2_done = sq.n

        def _(v):
            v.wait_ge(sq.h, rep2_done)
            last = None
            for dc in range(8):
                last = v.scalar_tensor_tensor(xn2v(dc), xall(dc), t_normw.ap()[:, dc:dc + 1], REP_a,
                                              ALU.mult, ALU.mult)
            sv.inc(last)
        block.vector(_)

        def _(t):
            wait(t, sv)
            stm.inc(t.transpose(P[7].ap()[0:1, 0:128], t_maxacc[:], t_eye32[:]))
        block.tensor(_)

        def _(v):
            wait(v, stm)
            sv.inc(v.reduce_max(t_small.ap()[0:1, 0:1], P[7].ap()[0:1, 0:128], axis=AX.X))
        block.vector(_)

        if stage >= 1:
            def ipw(dc):
                if dc < 7:
                    return A_C16[:, dc * FI:(dc + 1) * FI]
                return A_D16[:, 0:FI]

            CVI = [A_B16[:, 0:T + 64], A_B16[:, T + 64:2 * T + 128]]
            CVA = [A_B16[:, 2 * T + 128:3 * T + 128], A_B16[:, 3 * T + 128:4 * T + 128]]
            ZS = [A_B16[:, 4 * T + 128:5 * T + 128], A_B16[:, 5 * T + 128:6 * T + 128]]
            XSS = ZS

            gq_ip0 = gq.n

            def _(g):
                wait(g, sv, sq, stm, ss, sr2)
                for dc in range(8):
                    gq.dma(g.dma_start(out=ipw(dc), in_=inprojw[dc * 128:(dc + 1) * 128, :]))
            block.gpsimd(_)
            ipload = [gq_ip0 + 16 * (dc + 1) for dc in range(8)]

            def _(v):
                wait(v, gq)
                v.memset(CVI[0][:, 0:64], 0.0)
                sv.inc(v.memset(CVI[1][:, 0:64], 0.0))
            block.vector(_)

            st_ip0 = stm.n
            ss_ip0 = ss.n
            sv_cv0 = sv.n

            def _(t):
                t.wait_ge(gq.h, ipload[7])
                wait(t, sv)
                k = 0
                for fc in range(35):
                    fw = 128 if fc < 34 else 32
                    for nb in range(4):
                        bank = k % 8
                        if k >= 8:
                            t.wait_ge(ss.h, ss_ip0 + (k - 8) + 1)
                        last = None
                        for dc in range(8):
                            last = t.matmul(P[bank].ap()[0:fw, :], ipw(dc)[:, fc * 128:fc * 128 + fw],
                                            xn2v(dc)[:, nb * 512:(nb + 1) * 512],
                                            start=(dc == 0), stop=(dc == 7))
                        stm.inc(last)
                        k += 1
            block.tensor(_)

            def emit_silu(sc, cc):
                a1 = CVA[(cc + 1) % 2]
                sc.wait_ge(sv.h, sv_cv0 + 4 * (cc + 1))
                if cc < 16:
                    if cc < 2:
                        sc.wait_ge(szs[0].h, 16 * 8)
                        sc.wait_ge(szs[1].h, 16 * 8)
                    else:
                        sc.wait_ge(sxs[cc % 2].h, 16 * (cc // 2))
                    ssl.inc(sc.activation(out=XSS[cc % 2][:, 0:T], in_=a1[:, 0:T], func=AF.Silu, bias=0.0, scale=1.0))
                else:
                    ssl.inc(sc.activation(out=t_bc[:, (cc - 16) * T:(cc - 15) * T], in_=a1[:, 0:T],
                                          func=AF.Silu, bias=0.0, scale=1.0))

            def _(sc):
                k = 0
                for fc in range(34):
                    for nb in range(4):
                        bank = k % 8
                        sc.wait_ge(stm.h, st_ip0 + k + 1)
                        if fc < 16:
                            if fc >= 2 and nb == 0:
                                sc.wait_ge(szs[fc % 2].h, 16 * (fc // 2))
                            ss.inc(sc.activation(out=ZS[fc % 2][:, nb * 512:(nb + 1) * 512],
                                                 in_=P[bank].ap()[:, :], func=AF.Silu, bias=0.0, scale=1.0))
                        else:
                            cc = fc - 16
                            if cc >= 2 and nb == 0:
                                sc.wait_ge(sv.h, sv_cv0 + 4 * (cc - 1))
                            ss.inc(sc.activation(out=CVI[cc % 2][:, 64 + nb * 512:64 + (nb + 1) * 512],
                                                 in_=P[bank].ap()[:, :], func=AF.Copy, bias=0.0, scale=1.0))
                        k += 1
                    if fc >= 18:
                        emit_silu(sc, fc - 18)
                for cc in (16, 17):
                    emit_silu(sc, cc)
            block.scalar(_)


            def _(s):
                for fc in range(16):
                    s.wait_ge(ss.h, ss_ip0 + 4 * (fc + 1))
                    szs[fc % 2].dma(s.dma_start(out=d_z[fc * 128:(fc + 1) * 128, :], in_=ZS[fc % 2][:, 0:T]))
            block.sync(_)

            def _(v):
                for cc in range(18):
                    fc = 16 + cc
                    v.wait_ge(ss.h, ss_ip0 + 4 * (fc + 1))
                    if cc >= 1:
                        v.wait_ge(ssl.h, cc)
                    ci = CVI[cc % 2]
                    a0, a1 = CVA[cc % 2], CVA[(cc + 1) % 2]
                    v.tensor_scalar(a0[:, 0:T], ci[:, 61:61 + T], t_convw.ap()[:, 4 * cc:4 * cc + 1],
                                    t_convb.ap()[:, cc:cc + 1], ALU.mult, ALU.add)
                    v.scalar_tensor_tensor(a1[:, 0:T], ci[:, 62:62 + T], t_convw.ap()[:, 4 * cc + 1:4 * cc + 2],
                                           a0[:, 0:T], ALU.mult, ALU.add)
                    v.scalar_tensor_tensor(a0[:, 0:T], ci[:, 63:63 + T], t_convw.ap()[:, 4 * cc + 2:4 * cc + 3],
                                           a1[:, 0:T], ALU.mult, ALU.add)
                    sv.inc(v.scalar_tensor_tensor(a1[:, 0:T], ci[:, 64:64 + T], t_convw.ap()[:, 4 * cc + 3:4 * cc + 4],
                                                  a0[:, 0:T], ALU.mult, ALU.add), 4)
            block.vector(_)

            sv_dt0 = sv.n

            def _(v):
                for nb in range(4):
                    k = 136 + nb
                    v.wait_ge(stm.h, st_ip0 + k + 1)
                    sv.inc(v.tensor_scalar(t_hrows[0:32, nb * 512:(nb + 1) * 512], P[k % 8].ap()[0:32, :],
                                           t_dtb.ap()[:, 0:1], None, ALU.add))
            block.vector(_)

            def _(s):
                for cc in range(16):
                    s.wait_ge(ssl.h, cc + 1)
                    sxs[cc % 2].dma(s.dma_start(out=d_xs[cc * 128:(cc + 1) * 128, :], in_=XSS[cc % 2][:, 0:T]))
            block.sync(_)

            LL = A_D32[0:32, 3 * T:4 * T]

            def _(sc):
                sc.wait_ge(ss.h, ss_ip0 + 136)
                sc.wait_ge(sv.h, sv_dt0 + 4)
                sc.activation(out=LL, in_=t_hrows[0:32, :], func=AF.Abs, bias=0.0, scale=1.0)
                sc.activation(out=LL, in_=LL, func=AF.Exp, bias=0.0, scale=-1.0)
                ss.inc(sc.activation(out=LL, in_=LL, func=AF.Ln, bias=1.0, scale=1.0))
                ss.inc(sc.activation(out=t_eA[:, 0:1], in_=t_alogc[:, 0:1], func=AF.Exp, bias=0.0, scale=1.0))
            block.scalar(_)

            def _(v):
                wait(v, ss)
                sv.inc(v.scalar_tensor_tensor(t_hrows[0:32, :], t_hrows[0:32, :], 0.0, LL,
                                              ALU.max, ALU.add))
                sv.inc(v.tensor_scalar(t_hrows[32:64, :], t_hrows[0:32, :], t_eA.ap()[:, 0:1], -1.0,
                                       ALU.mult, ALU.mult))
                last = None
                for c in range(NCH):
                    sl = slice(c * CH, (c + 1) * CH)
                    last = v.tensor_tensor_scan(t_hrows[64:96, sl], t_hrows[32:64, sl], t_hrows[32:64, sl],
                                                0.0, ALU.add, ALU.bypass)
                sv.inc(last)
            block.vector(_)
            dt_ready = sv.n

            def _(s):
                wait(s, sv)
                sq.dma(s.dma_start(out=d_cs[:, :], in_=t_hrows[64:96, :]))
            block.sync(_)

            st_tm0 = stm.n

            def _(t):
                wait(t, sv)
                for c in range(NCH):
                    bank = c % 2
                    if c >= 2:
                        t.wait_ge(sv.h, dt_ready + (c - 1))
                    stm.inc(t.transpose(P[bank].ap()[0:128, 0:96], t_hrows[0:96, c * CH:(c + 1) * CH], t_eye32[0:96, 0:96]))
            block.tensor(_)

            def _(v):
                for c in range(NCH):
                    v.wait_ge(stm.h, st_tm0 + c + 1)
                    sv.inc(v.tensor_copy(t_tm.ap()[:, c * 96:c * 96 + 96], P[c % 2].ap()[:, 0:96]))
            block.vector(_)

            cstm_pat = t_tm.ap().rearrange("p (c q) -> p c q", q=96)[:, :, 64:96]

            def _(s):
                wait(s, sv)
                srcS = t_tm.ap()[127:128, :].rearrange("p (c q) -> p c q", q=96)[:, :, 64:96]
                sq.dma(s.dma_start(out=t_esr[0:1, :], in_=srcS))
            block.sync(_)

            def _(sc):
                wait(sc, sq)
                ss.inc(sc.activation(out=t_esr[32:33, :], in_=t_esr[0:1, :], func=AF.Exp, bias=0.0, scale=1.0))
            block.scalar(_)

            def _(s):
                wait(s, ss)
                sq.dma(s.dma_start(out=d_esr[:], in_=t_esr[32:33, :]))
                sq.dma(s.dma_start(out=A_D32[:, 3 * T:3 * T + 512], in_=t_esr[0:1, :].unsqueeze(1).to_broadcast([1, 128, NCH * H])))
            block.sync(_)

            def _(v):
                wait(v, sq)
                sv.inc(v.tensor_tensor(A_D32[:, 3 * T + 512:3 * T + 1024].rearrange("p (c q) -> p c q", q=32), A_D32[:, 3 * T:3 * T + 512].rearrange("p (c q) -> p c q", q=32), cstm_pat, ALU.subtract))
            block.vector(_)

            def _(sc):
                wait(sc, sv)
                ss.inc(sc.activation(out=t_dall[:], in_=A_D32[:, 3 * T + 512:3 * T + 1024], func=AF.Exp, bias=0.0, scale=1.0))
            block.scalar(_)

            def _(s):
                wait(s, ss)
            block.sync(_)

        if stage >= 2:
            XSTB = A_C16[:, 0:8192]
            ZTB = A_C16[:, 8192:16384]
            XDTM = A_C16[:, 16384:18432]
            XD2 = A_C16[:, 18432:20480]
            Y1 = A_C16[:, 20480:28672]
            HID = A_C16[:, 16384:17408]
            XNTB = A_B32[:, 0:4096]
            SUS = A_B32[:, 4096:4608]
            PREV = A_B16[:, 9216:11264]
            PREVS = A_B16[:, 11264:13312]
            YOFFS = A_B16[:, 13312:15360]
            BTM = A_B16[:, 15360:15488]
            CBT = A_B16[:, 15488:15616]
            XSD = A_B16[:, 15616:16128]
            CSROW = A_D32[:, 0:2048]
            SEG = A_D32[:, 2048:4096]
            EE = A_C16[:, 30720:32768]
            ESREP = A_D32[:, 6144:6176]
            MT = A_C16[:, 28672:30720]
            d_csr = d_cs.rearrange("h (c l) -> c h l", l=CH)
            d_esx = d_esr.rearrange("(c h) -> c h", h=H)

            def _(v):
                wait(v, sq, gq)
                v.memset(PREV[:, :], 0.0)
                v.memset(t_scores[:], 0.0)
                sv.inc(v.memset(t_sstat[:], 0.0))
            block.vector(_)

            xtg = 0
            opg = 0

            for tb in range(NTB):
                def _(s, tb=tb):
                    wait(s, sv, ss, stm)
                    for fi in range(16):
                        sq.dma(s.dma_start(out=XSTB[:, fi * 512:(fi + 1) * 512],
                                           in_=d_xs[fi * 128:(fi + 1) * 128, tb * TBS:(tb + 1) * TBS]))
                    for fi in range(16):
                        sq.dma(s.dma_start(out=ZTB[:, fi * 512:(fi + 1) * 512],
                                           in_=d_z[fi * 128:(fi + 1) * 128, tb * TBS:(tb + 1) * TBS]))
                    for dc in range(8):
                        sq.dma(s.dma_start(out=XNTB[:, dc * 512:(dc + 1) * 512],
                                           in_=d_xn[dc * 128:(dc + 1) * 128, tb * TBS:(tb + 1) * TBS]))
                block.sync(_)
                tb_loaded = sq.n

                for c4 in range(4):
                    c = tb * 4 + c4
                    ccol = c * 128
                    xtg_base = xtg

                    def _(t, c4=c4, xtg_base=xtg_base):
                        if c4 == 0:
                            t.wait_ge(sq.h, tb_loaded)
                        wait(t, sv)
                        for grp in range(4):
                            bank = grp % 2
                            if grp >= 2:
                                t.wait_ge(sxt.h, xtg_base + (grp - 2) + 1)
                            last = None
                            for j in range(4):
                                fi = grp * 4 + j
                                last = t.matmul(PB[bank][0:128, j * 128:(j + 1) * 128],
                                                XSTB[:, fi * 512 + c4 * 128:fi * 512 + c4 * 128 + 128],
                                                t_eye16[:], is_transpose=True, start=True, stop=True)
                            stm.inc(last)
                    block.tensor(_)
                    st_xt = stm.n - 4

                    def _(v, c=c, ccol=ccol, st_xt=st_xt):
                        for grp in range(4):
                            bank = grp % 2
                            v.wait_ge(stm.h, st_xt + grp + 1)
                            dtp = bcast_pat(t_tm.ap()[:, c * 96 + 8 * grp:c * 96 + 8 * grp + 8], PD)
                            sxt.inc(v.tensor_tensor(XDTM[:, grp * 512:(grp + 1) * 512].rearrange("p (a b) -> p a b", b=PD),
                                                    PB[bank][:, 0:512].rearrange("p (a b) -> p a b", b=PD), dtp, ALU.mult))
                        dap = bcast_pat(t_dall.ap()[:, c * H:(c + 1) * H], PD)
                        sv.inc(v.tensor_tensor(XD2.rearrange("p (a b) -> p a b", b=PD),
                                               XDTM.rearrange("p (a b) -> p a b", b=PD), dap, ALU.mult))
                    block.vector(_)
                    xtg += 4

                    def _(t, ccol=ccol):
                        wait(t, sv)
                        stm.inc(t.transpose(PB[2][0:128, 0:128], t_bc.ap()[:, ccol:ccol + 128], t_eye16[:]))
                    block.tensor(_)

                    def _(v):
                        wait(v, stm)
                        sv.inc(v.tensor_copy(BTM[:, :], PB[2][:, 0:128]))
                    block.vector(_)

                    def _(t, ccol=ccol):
                        wait(t, sv)
                        stm.inc(t.matmul(P[2].ap()[:, 0:128], t_bc.ap()[:, ccol:ccol + 128],
                                         t_bc.ap()[:, T + ccol:T + ccol + 128], start=True, stop=True))
                    block.tensor(_)

                    def _(v):
                        wait(v, stm)
                        sv.inc(v.tensor_tensor(CBT[:, :], P[2].ap()[:, 0:128], t_tri16.ap()[:, :], ALU.mult))
                    block.vector(_)

                    def _(sc, c=c):
                        wait(sc, sv)
                        ss.inc(sc.activation(out=t_ecsc[:, :], in_=t_tm.ap()[:, c * 96 + 64:c * 96 + 96],
                                             func=AF.Exp, bias=0.0, scale=1.0))
                    block.scalar(_)

                    def _(t, ccol=ccol):
                        wait(t, sv, ss)
                        last = None
                        for h in range(H):
                            last = t.matmul(P[h // 8].ap()[:, (h % 8) * 64:(h % 8) * 64 + 64],
                                            t_bc.ap()[:, T + ccol:T + ccol + 128],
                                            PREV[:, h * 64:(h + 1) * 64], start=True, stop=True)
                        stm.inc(last)
                    block.tensor(_)

                    def _(v):
                        wait(v, stm)
                        last = None
                        for b in range(4):
                            ep = bcast_pat(t_ecsc.ap()[:, b * 8:(b + 1) * 8], PD)
                            last = v.tensor_tensor(YOFFS[:, b * 512:(b + 1) * 512].rearrange("p (a b) -> p a b", b=PD),
                                                   P[b].ap()[:, :].rearrange("p (a b) -> p a b", b=PD), ep, ALU.mult)
                        sv.inc(last)
                    block.vector(_)

                    def _(t):
                        wait(t, sv)
                        last = None
                        for h in range(H):
                            last = t.matmul(P[h // 8].ap()[:, (h % 8) * 64:(h % 8) * 64 + 64],
                                            BTM[:, :], XD2[:, h * 64:(h + 1) * 64], start=True, stop=True)
                        stm.inc(last)
                    block.tensor(_)
                    st_states = stm.n

                    def _(s, c=c):
                        wait(s, sv)
                        sq.dma(s.dma_start(out=ESREP[:, :], in_=d_esx[c:c + 1, :].partition_broadcast(128)))
                    block.sync(_)

                    def _(v):
                        wait(v, stm, sq)
                        v.tensor_tensor(PREVS.rearrange("p (a b) -> p a b", b=PD),
                                        PREV.rearrange("p (a b) -> p a b", b=PD),
                                        bcast_pat(ESREP[:, 0:32], PD), ALU.mult)
                        last = None
                        for b in range(4):
                            last = v.tensor_tensor(PREV[:, b * 512:(b + 1) * 512], PREVS[:, b * 512:(b + 1) * 512],
                                                   P[b].ap()[:, :], ALU.add)
                        sv.inc(last)
                    block.vector(_)

                    for half in range(2):
                        sq_cs = sq.n

                        def _(s, c=c, half=half):
                            wait(s, sv, ss)
                            sq.dma(s.dma_start(out=CSROW[:, :],
                                               in_=d_csr[c:c + 1, 16 * half:16 * half + 16, :].partition_broadcast(128)))
                        block.sync(_)

                        def _(v, c=c, half=half):
                            wait(v, sq)
                            cpat = bcast_pat(t_tm.ap()[:, c * 96 + 64 + 16 * half:c * 96 + 64 + 16 * half + 16], CH)
                            v.tensor_tensor(SEG.rearrange("p (a b) -> p a b", b=CH),
                                            CSROW.rearrange("p (a b) -> p a b", b=CH), cpat, ALU.subtract)
                            sv.inc(v.tensor_scalar(SEG[:, :], SEG[:, :], 0.0, None, ALU.min))
                        block.vector(_)

                        def _(sc):
                            wait(sc, sv)
                            ss.inc(sc.activation(out=EE[:, :], in_=SEG[:, :], func=AF.Exp, bias=0.0, scale=1.0))
                        block.scalar(_)

                        def _(v, half=half):
                            wait(v, ss)
                            cbtpat = CBT.unsqueeze(1).to_broadcast([128, 16, 128])
                            sv.inc(v.tensor_tensor(MT.rearrange("p (a b) -> p a b", b=128),
                                                   EE.rearrange("p (a b) -> p a b", b=128), cbtpat, ALU.mult))
                        block.vector(_)

                        def _(t, half=half):
                            wait(t, sv)
                            last = None
                            for hl in range(16):
                                h = half * 16 + hl
                                i = h // 2
                                bank, colo, po = 4 + i // 4, (i % 4) * 128, (h % 2) * 64
                                tp = (0, po)
                                t.matmul(P[bank].ap()[po:po + 64, colo:colo + 128],
                                         XDTM[:, h * 64:(h + 1) * 64], MT[:, hl * 128:(hl + 1) * 128],
                                         start=True, stop=False, tile_position=tp)
                                last = t.matmul(P[bank].ap()[po:po + 64, colo:colo + 128],
                                                YOFFS[:, h * 64:(h + 1) * 64], t_eye16[:],
                                                start=False, stop=True, tile_position=tp)
                            stm.inc(last)
                        block.tensor(_)

                    def _(v, c4=c4):
                        wait(v, stm)
                        last = None
                        for b in range(4):
                            xs_str = XSTB.rearrange("p (fi t) -> p fi t", t=512)[:, 4 * b:4 * b + 4, c4 * 128:(c4 + 1) * 128]
                            dpat = bcast_pat(t_dpp.ap()[:, 4 * b:4 * b + 4], 128)
                            v.tensor_tensor(XSD.rearrange("p (a b) -> p a b", b=128), xs_str, dpat, ALU.mult)
                            y1_str = Y1.rearrange("p (fi t) -> p fi t", t=512)[:, 4 * b:4 * b + 4, c4 * 128:(c4 + 1) * 128]
                            last = v.tensor_tensor(y1_str, XSD.rearrange("p (a b) -> p a b", b=128), P[4 + b].ap()[:, :].rearrange("p (a b) -> p a b", b=128), ALU.add)
                        sv.inc(last)
                    block.vector(_)

                def _(v):
                    wait(v, stm)
                    sv.inc(v.tensor_tensor(XSTB[:, :], Y1[:, :], ZTB[:, :], ALU.mult))
                block.vector(_)

                def _(sc):
                    wait(sc, sv)
                    ss.inc(sc.activation(out=ZTB[:, :], in_=XSTB[:, :], func=AF.Square, bias=0.0, scale=1.0))
                block.scalar(_)

                def _(t):
                    wait(t, ss)
                    last = None
                    for fi in range(16):
                        last = t.matmul(P[0].ap()[0:1, 0:512], t_ones16.ap()[:, :], ZTB[:, fi * 512:(fi + 1) * 512],
                                        start=(fi == 0), stop=(fi == 15))
                    stm.inc(last)
                block.tensor(_)

                def _(v):
                    wait(v, stm)
                    sv.inc(v.tensor_scalar(SROWB, P[0].ap()[0:1, 0:512], 1.0 / DI, 1e-5, ALU.mult, ALU.add))
                    v.wait_ge(sv.h, sv.n)
                    sv.inc(v.reciprocal(SROW, SROWB))
                block.vector(_)

                def _(sc):
                    wait(sc, sv)
                    ss.inc(sc.activation(out=t_r3b[0:1, :], in_=SROW, func=AF.Sqrt, bias=0.0, scale=1.0))
                block.scalar(_)

                def _(t):
                    wait(t, ss)
                    stm.inc(t.matmul(P[1].ap()[:, 0:512], t_onesr16.ap()[:, :], t_r3b.ap()[0:1, :], start=True, stop=True))
                block.tensor(_)

                def _(v):
                    wait(v, stm)
                    rp = P[1].ap().unsqueeze(1).to_broadcast([128, 16, 512])
                    v.tensor_tensor(ZTB.rearrange("p (a b) -> p a b", b=512),
                                    XSTB.rearrange("p (a b) -> p a b", b=512), rp, ALU.mult)
                    wp = bcast_pat(t_ssmw.ap()[:, 0:16], 512)
                    sv.inc(v.tensor_tensor(Y1.rearrange("p (a b) -> p a b", b=512),
                                           ZTB.rearrange("p (a b) -> p a b", b=512), wp, ALU.mult))
                block.vector(_)

                st_op0 = stm.n
                opg_base = opg
                OPW = [A_C16[:, 16384:18432], A_C16[:, 18432:20480]]
                gq_op0 = gq.n

                def _(g):
                    wait(g, sv)
                    for dc in range(8):
                        if dc >= 2:
                            g.wait_ge(stm.h, st_op0 + (dc - 1))
                        gq.dma(g.dma_start(out=OPW[dc % 2].rearrange("p (c f) -> p c f", f=128),
                                           in_=outpw[:, dc * 128:(dc + 1) * 128].rearrange("(c p) f -> p c f", p=128)))
                block.gpsimd(_)

                def _(t, opg_base=opg_base, gq_op0=gq_op0):
                    wait(t, sv)
                    for dc in range(8):
                        bank = 2 + dc % 2
                        t.wait_ge(gq.h, gq_op0 + 16 * (dc + 1))
                        if dc >= 2:
                            t.wait_ge(sop.h, opg_base + (dc - 2) + 1)
                        last = None
                        for fi in range(16):
                            last = t.matmul(P[bank].ap()[:, 0:512], OPW[dc % 2][:, fi * 128:(fi + 1) * 128],
                                            Y1[:, fi * 512:(fi + 1) * 512], start=(fi == 0), stop=(fi == 15))
                        stm.inc(last)
                block.tensor(_)

                ss_y20 = ss.n

                def _(sc, tb=tb):
                    for dc in range(8):
                        bank = 2 + dc % 2
                        sc.wait_ge(stm.h, st_op0 + dc + 1)
                        ss.inc(sc.activation(out=A_A[:, dc * T + tb * TBS:dc * T + (tb + 1) * TBS],
                                             in_=P[bank].ap()[:, 0:512], func=AF.Copy, bias=0.0, scale=1.0))
                block.scalar(_)

                ssu_base = ssu.n

                def _(v, opg_base=opg_base, ssu_base=ssu_base):
                    for dc in range(8):
                        bank = 2 + dc % 2
                        v.wait_ge(ss.h, ss_y20 + dc + 1)
                        if dc >= 1:
                            v.wait_ge(ssu.h, ssu_base + 16 * dc)
                        sop.inc(v.tensor_tensor(SUS[:, 0:512], P[bank].ap()[:, 0:512],
                                                XNTB[:, dc * 512:(dc + 1) * 512], ALU.add))
                block.vector(_)
                sop_b2 = sop.n - 8
                opg += 8

                def _(s, tb=tb, sop_b2=sop_b2):
                    for dc in range(8):
                        s.wait_ge(sop.h, sop_b2 + dc + 1)
                        ssu.dma(s.dma_start(out=d_su[dc * 128:(dc + 1) * 128, tb * TBS:(tb + 1) * TBS], in_=SUS[:, 0:512]))
                block.sync(_)

                def _(t, tb=tb):
                    wait(t, ss, sv)
                    for hc in range(2):
                        last = None
                        for dc in range(8):
                            last = t.matmul(P[4 + hc].ap()[:, 0:512],
                                            t_scw.ap()[:, dc * 2 * 128 + hc * 128:dc * 2 * 128 + hc * 128 + 128],
                                            A_A[:, dc * T + tb * TBS:dc * T + (tb + 1) * TBS],
                                            start=(dc == 0), stop=(dc == 7))
                        stm.inc(last)
                block.tensor(_)

                def _(v):
                    wait(v, stm)
                    last = None
                    for hc in range(2):
                        last = v.tensor_scalar(HID[:, hc * 512:(hc + 1) * 512], P[4 + hc].ap()[:, 0:512], 0.0, None, ALU.max)
                    sv.inc(last)
                block.vector(_)

                def _(t):
                    wait(t, sv)
                    last = None
                    for hc in range(2):
                        last = t.matmul(P[6].ap()[0:1, 0:512], t_scw2.ap()[:, hc:hc + 1], HID[:, hc * 512:(hc + 1) * 512],
                                        start=(hc == 0), stop=(hc == 1))
                    stm.inc(last)
                block.tensor(_)

                def _(sc, tb=tb):
                    wait(sc, stm)
                    ss.inc(sc.activation(out=t_scores[32 * tb:32 * tb + 1, :], in_=P[6].ap()[0:1, 0:512], func=AF.Sigmoid, bias=0.0, scale=1.0))
                block.scalar(_)

                st_sum0 = stm.n

                ssm_base = ssm.n

                def _(t, tb=tb, ssm_base=ssm_base):
                    wait(t, sv, ss)
                    for tc4 in range(4):
                        bank = 5 + tc4 % 2
                        if tc4 >= 2:
                            t.wait_ge(ssm.h, ssm_base + (tc4 - 2) + 1)
                        last = None
                        for dc in range(8):
                            last = t.matmul(P[bank].ap()[0:128, 0:64],
                                            A_A[:, dc * T + tb * TBS + tc4 * 128:dc * T + tb * TBS + (tc4 + 1) * 128],
                                            t_sumw.ap()[:, dc * 64:(dc + 1) * 64], start=(dc == 0), stop=(dc == 7))
                        stm.inc(last)
                    last = None
                    for dc in range(8):
                        last = t.matmul(P[4].ap()[0:64, 0:512], t_qww.ap()[:, dc * 64:(dc + 1) * 64],
                                        A_A[:, dc * T + tb * TBS:dc * T + (tb + 1) * TBS], start=(dc == 0), stop=(dc == 7))
                    stm.inc(last)
                block.tensor(_)

                def _(v, tb=tb, st_sum0=st_sum0):
                    for tc4 in range(4):
                        bank = 5 + tc4 % 2
                        v.wait_ge(stm.h, st_sum0 + tc4 + 1)
                        ssm.inc(v.tensor_copy(t_summtm.ap()[:, (tb * 4 + tc4) * 64:(tb * 4 + tc4 + 1) * 64],
                                              P[bank].ap()[0:128, 0:64]))
                    v.wait_ge(stm.h, st_sum0 + 5)
                    ssm.inc(v.tensor_copy(t_qT.ap()[:, tb * TBS:(tb + 1) * TBS], P[4].ap()[0:64, 0:512]))
                block.vector(_)

        if stage >= 3:
            SCWA = A_C32[:, 0:512]
            SCWB = A_C32[:, 512:1024]
            CAND8 = A_C32[:, 1024:1032]
            CANDA = A_C32[:, 1040:1096]
            RL224 = A_C32[0:1, 1096:1320]
            RL224B = A_C32[0:1, 1320:1544]
            RG8 = A_C32[0:1, 1544:1552]
            RLOC = A_C32[0:1, 1552:1608]
            RSTAT = A_C32[0:1, 1616:1632]
            SELT = [t_selt.ap()[:, 0:NLOC], A_C16[:, 3328:3328 + NLOC]]

            def _(v):
                wait(v, sv, ss, stm, sq, ssm)
                v.tensor_scalar(SCWA, t_scores[:], 0.5, None, ALU.is_gt)
                v.reduce_sum(t_maxcol[:], SCWA, axis=AX.X)
                sv.inc(v.reduce_sum(t_maxacc[:], t_scores[:], axis=AX.X))
            block.vector(_)

            def _(t):
                wait(t, sv)
                t.transpose(P[7].ap()[0:1, 0:128], t_maxcol[:], t_eye32[:])
                stm.inc(t.transpose(P[6].ap()[0:1, 0:128], t_maxacc[:], t_eye32[:]))
            block.tensor(_)

            def _(v):
                wait(v, stm)
                v.reduce_sum(RSTAT[:, 0:1], P[7].ap()[0:1, 0:128], axis=AX.X)
                sv.inc(v.reduce_sum(RSTAT[:, 1:2], P[6].ap()[0:1, 0:128], axis=AX.X))
                v.wait_ge(sv.h, sv.n)
                sv.inc(v.tensor_copy(RSTAT[:, 2:3], t_small.ap()[0:1, 0:1]))
                v.tensor_copy(SCWA, t_scores[:])
                cur, other = SCWA, SCWB
                for r in range(7):
                    sv.inc(v.max(CAND8, cur))
                    v.wait_ge(sv.h, sv.n)
                    v.tensor_copy(CANDA[:, r * 8:(r + 1) * 8], CAND8)
                    v.match_replace(other, CAND8, cur, -1.0)
                    cur, other = other, cur
                sv.inc(v.engine_nop() if False else v.tensor_copy(t_maxcol[:], CAND8[:, 0:1]))
            block.vector(_)

            def _(s):
                wait(s, sv)
                for i in range(4):
                    sq.dma(s.dma_start(out=RL224[:, i * 56:(i + 1) * 56], in_=CANDA[32 * i:32 * i + 1, :]))
            block.sync(_)

            def _(v):
                wait(v, sq)
                cur, other = RL224, RL224B
                for r in range(7):
                    sv.inc(v.max(RG8, cur))
                    v.wait_ge(sv.h, sv.n)
                    v.tensor_copy(RLOC[:, r * 8:(r + 1) * 8], RG8)
                    v.match_replace(other, RG8, cur, -1.0)
                    cur, other = other, cur
                sv.inc(v.tensor_copy(t_small.ap()[0:1, 8:9], RG8[:, 0:1]))
            block.vector(_)

            def _(s):
                wait(s, sv)
                sq.dma(s.dma_start(out=d_row[3:4, 0:56], in_=RLOC))
                s.wait_ge(sq.h, sq.n)
                sq.dma(s.dma_start(out=t_candrep[:], in_=d_row[3:4, 0:56].partition_broadcast(128)))
                for tb4 in range(4):
                    sq.dma(s.dma_start(out=d_row[1:2, tb4 * 512:(tb4 + 1) * 512],
                                       in_=t_scores[32 * tb4:32 * tb4 + 1, :]))
                s.wait_ge(sq.h, sq.n)
                sq.dma(s.dma_start(out=t_sctm[:], in_=d_row[1:2, 0:T].rearrange("one (c p) -> (one p) c", p=128),
                                   allow_slow_non_contiguous=True))
            block.sync(_)
            sctm_done = sq.n

            st_pl0 = stm.n

            def _(v, st_pl0=st_pl0):
                v.wait_ge(sq.h, sctm_done)
                for tc in range(16):
                    if tc >= 2:
                        v.wait_ge(stm.h, st_pl0 + (tc - 1))
                    sv.inc(v.tensor_scalar(SELT[tc % 2], t_candrep.ap(), t_sctm.ap()[:, tc:tc + 1], None, ALU.is_equal))
            block.vector(_)
            selt0 = sv.n - 16

            def _(t):
                for tc in range(16):
                    t.wait_ge(sv.h, selt0 + tc + 1)
                    stm.inc(t.matmul(P[5].ap()[0:PD, 0:NLOC], t_summtm.ap()[:, tc * PD:(tc + 1) * PD], SELT[tc % 2],
                                     start=(tc == 0), stop=(tc == 15)))
            block.tensor(_)

            def _(v):
                wait(v, stm)
                sv.inc(v.tensor_copy(t_pool.ap(), P[5].ap()[0:PD, 0:NLOC]))
            block.vector(_)

            def _(v):
                sv.inc(v.memset(A_C32[0:1, 1608:1616], 0.0))
            block.vector(_)

            def _(s):
                wait(s, sv)
                sq.dma(s.dma_start(out=cc_in[0:1, 59:64], in_=A_C32[0:1, 1608:1613]))
                sq.dma(s.dma_start(out=cc_in[0:1, 0:NLOC], in_=RLOC))
                sq.dma(s.dma_start(out=cc_in[0:1, 64:64 + PD * NLOC].rearrange("one (j r) -> (one j) r", r=NLOC),
                                   in_=t_pool.ap()))
                sq.dma(s.dma_start(out=cc_in[0:1, 56:59], in_=RSTAT[:, 0:3]))
                sq.dma(s.dma_start(out=cc_in[0:1, 3648:3904].rearrange("one (p c) -> (one p) c", p=128),
                                   in_=t_sca[:, 0:2]))
            block.sync(_)
            pay_done = sq.n

            def _(g):
                g.wait_ge(sq.h, pay_done)
                wait(g, gq)
                scc.inc(g.collective_compute("AllGather", ALU.bypass, replica_groups=[list(range(NCORES))],
                                             ins=[cc_in.opt()], outs=[cc_out.opt()]))
                g.wait_ge(scc.h, scc.n)
                gq.dma(g.dma_start(out=A_D32[0:1, 0:NCAND], in_=cc_out[:, 0:NLOC]))
                gq.dma(g.dma_start(out=t_poolall.ap().rearrange("j (r c) -> j r c", c=NLOC),
                                   in_=cc_out[:, 64:64 + PD * NLOC].rearrange("r (j c) -> j r c", c=NLOC),
                                   allow_slow_non_contiguous=True))
                for _rep in range(2):
                    gq.dma(g.dma_start(out=t_small.ap()[0:1, 16:24], in_=cc_out[:, 56:57], allow_slow_non_contiguous=True))
                    gq.dma(g.dma_start(out=t_small.ap()[0:1, 24:32], in_=cc_out[:, 57:58], allow_slow_non_contiguous=True))
                    gq.dma(g.dma_start(out=t_small.ap()[0:1, 32:40], in_=cc_out[:, 58:59], allow_slow_non_contiguous=True))
            block.gpsimd(_)
            ag1_done = gq.n

            def _(v):
                v.wait_ge(gq.h, ag1_done)
                v.reduce_max(t_small.ap()[0:1, 1:2], t_small.ap()[0:1, 32:40], axis=AX.X)
                v.reduce_sum(t_small.ap()[0:1, 2:3], t_small.ap()[0:1, 16:24], axis=AX.X)
                sv.inc(v.reduce_sum(t_small.ap()[0:1, 3:4], t_small.ap()[0:1, 24:32], axis=AX.X))
                v.wait_ge(sv.h, sv.n)
                sv.inc(v.tensor_scalar(t_small.ap()[0:1, 4:5], t_small.ap()[0:1, 1:2], 1e-8, None, ALU.max))
                v.wait_ge(sv.h, sv.n)
                sv.inc(v.tensor_scalar(t_small.ap()[0:1, 5:6], t_small.ap()[0:1, 4:5], 1.0 / 255.0, None, ALU.mult))
                v.wait_ge(sv.h, sv.n)
                sv.inc(v.reciprocal(t_small.ap()[0:1, 6:7], t_small.ap()[0:1, 5:6]))
                v.wait_ge(sv.h, sv.n)
                v.tensor_scalar(t_small.ap()[0:1, 9:10], t_small.ap()[0:1, 3:4], 1.0 / 16384.0, 0.3, ALU.mult, ALU.is_gt)
                sv.inc(v.tensor_scalar(t_small.ap()[0:1, 10:11], t_small.ap()[0:1, 2:3], 0.0, None, ALU.is_gt))
                v.wait_ge(sv.h, sv.n)
                sv.inc(v.tensor_tensor(t_small.ap()[0:1, 11:12], t_small.ap()[0:1, 9:10], t_small.ap()[0:1, 10:11], ALU.mult))
            block.vector(_)

            def _(s):
                wait(s, sv)
                sq.dma(s.dma_start(out=d_row[3:4, 100:112], in_=t_small.ap()[0:1, 0:12]))
                s.wait_ge(sq.h, sq.n)
                sq.dma(s.dma_start(out=t_sca[:, 0:1], in_=d_row[3:4, 106:107].partition_broadcast(128)))
                sq.dma(s.dma_start(out=t_sca[:, 1:2], in_=d_row[3:4, 105:106].partition_broadcast(128)))
                sq.dma(s.dma_start(out=t_sca[:, 4:5], in_=d_row[3:4, 111:112].partition_broadcast(128)))
                sq.dma(s.dma_start(out=d_row[2:3, 0:NCAND], in_=A_D32[0:1, 0:NCAND]))
            block.sync(_)
            sca1_done = sq.n

        if stage >= 4:
            WS = [A_C32[:, 0:4096], A_C32[:, 4096:8192]]
            WABS = A_C32[:, 8192:12288]
            WINV = A_C32[:, 12288:16384]
            WI32 = A_CI[:, 8192:12288]
            WQO = [A_D16[:, 0:4096], A_D16[:, 4096:8192]]
            sqw = Sem(nc, ctx, "sqw")

            for wi, (wsrc, wdst, dsw, nch, chw, swcol, nsw) in enumerate([
                    (w1t, d_wq1, d_sw1, 8, FFD, t_sw1, 32),
                    (w2t, d_wq2, d_sw2, 32, D, t_sw2, 8)]):
                sq_w0 = sq.n
                sg_w0 = sg.n

                sv_w0 = sv.n

                def _(s, wsrc=wsrc, nch=nch, sv_w0=sv_w0):
                    wait(s, sv, ss, stm, sg, sq, sqw)
                    for ch in range(nch):
                        if ch >= 2:
                            s.wait_ge(sv.h, sv_w0 + (ch - 1))
                        sq.dma(s.dma_start(out=WS[ch % 2][:, 0:chw], in_=wsrc[ch * 128:(ch + 1) * 128, :]))
                    sq.dma(s.dma_start(out=WS[(nch - 1) % 2][:, 0:chw], in_=wsrc[(nch - 1) * 128:nch * 128, :]))
                block.sync(_)
                wload = [sq_w0 + 16 * (ch + 2) for ch in range(nch)]

                WMIN = WINV

                def _(v, nch=nch, chw=chw, wload=wload):
                    for ch in range(nch):
                        v.wait_ge(sq.h, wload[ch])
                        if ch == 0:
                            v.tensor_scalar(WABS[:, 0:chw], WS[0][:, 0:chw], 1.0, None, ALU.mult)
                            sv.inc(v.tensor_scalar(WMIN[:, 0:chw], WS[0][:, 0:chw], 1.0, None, ALU.mult))
                        else:
                            v.tensor_tensor(WABS[:, 0:chw], WS[ch % 2][:, 0:chw], WABS[:, 0:chw], ALU.max)
                            sv.inc(v.tensor_tensor(WMIN[:, 0:chw], WS[ch % 2][:, 0:chw], WMIN[:, 0:chw], ALU.min))
                    v.tensor_scalar(WMIN[:, 0:chw], WMIN[:, 0:chw], -1.0, None, ALU.mult)
                    sv.inc(v.tensor_tensor(WABS[:, 0:chw], WMIN[:, 0:chw], WABS[:, 0:chw], ALU.max))
                block.vector(_)

                st_wt0 = stm.n
                sv_wt0 = sv.n

                def _(t, chw=chw, sv_wt0=sv_wt0):
                    wait(t, sv)
                    for j in range(chw // 128):
                        bank = j % 2
                        if j >= 2:
                            t.wait_ge(sv.h, sv_wt0 + (j - 1))
                        stm.inc(t.transpose(P[bank].ap()[:, 0:128], WABS[:, j * 128:(j + 1) * 128], t_eye32[:]))
                block.tensor(_)

                def _(v, chw=chw, swcol=swcol, nsw=nsw, st_wt0=st_wt0):
                    for j in range(chw // 128):
                        bank = j % 2
                        v.wait_ge(stm.h, st_wt0 + j + 1)
                        sv.inc(v.reduce_max(swcol.ap()[:, j:j + 1], P[bank].ap()[:, 0:128], axis=AX.X))
                    v.wait_ge(sv.h, sv.n)
                    sv.inc(v.tensor_scalar(swcol.ap()[:, 0:nsw], swcol.ap()[:, 0:nsw], 1e-8, 1.0 / 127.0, ALU.max, ALU.mult))
                block.vector(_)

                def _(t, swcol=swcol, nsw=nsw):
                    wait(t, sv)
                    stm.inc(t.transpose(P[2].ap()[0:nsw, 0:128], swcol.ap()[:, 0:nsw], t_eye32[:]))
                block.tensor(_)

                def _(v, nsw=nsw):
                    wait(v, stm)
                    sv.inc(v.tensor_copy(SCWA[0:nsw, 0:128], P[2].ap()[0:nsw, 0:128]))
                    v.wait_ge(sv.h, sv.n)
                    sv.inc(v.reciprocal(SCWB[0:nsw, 0:128], SCWA[0:nsw, 0:128]))
                block.vector(_)

                def _(s, dsw=dsw, nsw=nsw, chw=chw):
                    wait(s, sv)
                    sq.dma(s.dma_start(out=dsw.rearrange("(c p) -> c p", p=128), in_=SCWB[0:nsw, 0:128]))
                    s.wait_ge(sq.h, sq.n)
                    sq.dma(s.dma_start(out=WINV[:, 0:chw], in_=dsw.unsqueeze(0).partition_broadcast(128)))
                block.sync(_)
                winv_done = sq.n

                sq_q0 = sq.n
                sv_q0 = sv.n
                sqw_b = sqw.n

                def _(s, wsrc=wsrc, wdst=wdst, nch=nch, chw=chw, winv_done=winv_done, sv_q0=sv_q0):
                    s.wait_ge(sq.h, winv_done)
                    for ch in range(nch + 1):
                        if ch < nch:
                            if ch >= 2:
                                s.wait_ge(sv.h, sv_q0 + (ch - 1))
                            sq.dma(s.dma_start(out=WS[ch % 2][:, 0:chw], in_=wsrc[ch * 128:(ch + 1) * 128, :]))
                        if ch == nch:
                            sq.dma(s.dma_start(out=WS[(nch - 1) % 2][:, 0:chw], in_=wsrc[(nch - 1) * 128:nch * 128, :]))
                        if ch >= 1:
                            s.wait_ge(sv.h, sv_q0 + ch)
                            sqw.dma(s.dma_start(out=wdst[(ch - 1) * 128:ch * 128, :], in_=WQO[(ch - 1) % 2][:, 0:chw]))
                block.sync(_)
                qload = [sq_q0 + 16 * (ch + 2) for ch in range(nch)]

                def _(v, nch=nch, chw=chw, qload=qload, sqw_b=sqw_b):
                    for ch in range(nch):
                        v.wait_ge(sq.h, qload[ch])
                        if ch >= 2:
                            v.wait_ge(sqw.h, sqw_b + 16 * (ch - 1))
                        v.tensor_tensor(WI32[:, 0:chw], WS[ch % 2][:, 0:chw], WINV[:, 0:chw], ALU.mult)
                        sv.inc(v.tensor_scalar(WQO[ch % 2][:, 0:chw], WI32[:, 0:chw], 127.0, -128.0, ALU.min, ALU.max))
                block.vector(_)

            XNS = [A_D32[:, 0:2048], A_D32[:, 2048:4096]]
            QI32 = A_DI[:, 4096:6144]
            HROW = [A_D32[:, 0:2048], A_D32[:, 2048:4096]]
            WQ1S = [A_D16[:, 12288:14336], A_D16[:, 14336:16384]]
            XQ = A_B16

            sv_xq0 = sv.n
            sq_xq = sq.n + 2 * 16

            def _(s):
                wait(s, sv, sq, sqw)
                s.wait_ge(sq.h, sca1_done)
                sq.dma(s.dma_start(out=t_bcol[:, 0:32], in_=b1v.rearrange("(c p) -> p c", p=128), allow_slow_non_contiguous=True))
                sq.dma(s.dma_start(out=t_bcol[:, 0:32], in_=b1v.rearrange("(c p) -> p c", p=128), allow_slow_non_contiguous=True))
                for dc in range(8):
                    if dc >= 2:
                        s.wait_ge(sv.h, sv_xq0 + (dc - 1))
                    sq.dma(s.dma_start(out=XNS[dc % 2][:, :], in_=d_xn[dc * 128:(dc + 1) * 128, :]))
                sq.dma(s.dma_start(out=XNS[7 % 2][:, :], in_=d_xn[7 * 128:8 * 128, :]))
            block.sync(_)

            def _(v):
                for dc in range(8):
                    v.wait_ge(sq.h, sq_xq + 16 * (dc + 2))
                    v.tensor_scalar(QI32, XNS[dc % 2][:, :], t_sca.ap()[:, 0:1], None, ALU.mult)
                    sv.inc(v.tensor_scalar(XQ[:, dc * T:(dc + 1) * T], QI32, 255.0, 0.0, ALU.min, ALU.max))
                v.wait_ge(sq.h, sq_xq + 9 * 16)
                sv.inc(v.tensor_tensor(t_scl1[:], t_sw1[:], t_sca.ap()[:, 1:2].to_broadcast([128, 32]), ALU.mult))
                v.wait_ge(sv.h, sv.n)
                sv.inc(v.reciprocal(t_bi32.ap().bitcast(F32), t_scl1[:]))
                v.wait_ge(sv.h, sv.n)
                sv.inc(v.tensor_tensor(t_bi32[:], t_bcol[:], t_bi32.ap().bitcast(F32), ALU.mult))
                v.wait_ge(sv.h, sv.n)
                sv.inc(v.tensor_tensor(t_b1q[:], t_bi32[:], t_scl1[:], ALU.mult))
            block.vector(_)
            xq_done = sv.n

            st_m1 = stm.n
            gq_w1 = gq.n
            shq = Sem(nc, ctx, "shq")
            sv_h0 = sv.n
            sg_h0 = sg.n

            def _(g, gq_w1=gq_w1):
                wait(g, gq, sqw)
                for fg in range(16):
                    if fg >= 2:
                        g.wait_ge(stm.h, st_m1 + (fg - 1))
                    gq.dma(g.dma_start(out=WQ1S[fg % 2].rearrange("p (c f) -> p c f", f=256),
                                       in_=d_wq1[:, fg * 256:(fg + 1) * 256].rearrange("(c p) f -> p c f", p=128)))
            block.gpsimd(_)

            def _(t, gq_w1=gq_w1):
                wait(t, sv)
                for fg in range(16):
                    t.wait_ge(gq.h, gq_w1 + 16 * (fg + 1))
                    if fg >= 1:
                        t.wait_ge(sv.h, sv_h0 + 2 * fg)
                    last = None
                    for dch in range(8):
                        for fc in range(2):
                            for tbk in range(4):
                                bank = fc * 4 + tbk
                                last = t.matmul(P[bank].ap()[:, :],
                                                WQ1S[fg % 2][:, dch * 256 + fc * 128:dch * 256 + fc * 128 + 128],
                                                XQ[:, dch * T + tbk * 512:dch * T + (tbk + 1) * 512],
                                                start=(dch == 0), stop=(dch == 7))
                    stm.inc(last)
            block.tensor(_)

            def _(v):
                for fg in range(16):
                    v.wait_ge(stm.h, st_m1 + fg + 1)
                    if fg >= 1:
                        v.wait_ge(shq.h, 32 * fg)
                    for fc in range(2):
                        f = fg * 2 + fc
                        for tbk in range(4):
                            bank = fc * 4 + tbk
                            v.tensor_scalar(HROW[fc][:, tbk * 512:(tbk + 1) * 512], P[bank].ap()[:, :],
                                            t_scl1.ap()[:, f:f + 1], t_b1q.ap()[:, f:f + 1], ALU.mult, ALU.add)
                        sv.inc(v.reduce_max(t_summtm.ap()[:, f:f + 1], HROW[fc][:, :], axis=AX.X))
                v.wait_ge(sv.h, sv.n)
                sv.inc(v.reduce_max(t_maxacc[:], t_summtm.ap()[:, 0:32], axis=AX.X))
            block.vector(_)

            def _(s):
                for fg in range(16):
                    for fc in range(2):
                        f = fg * 2 + fc
                        s.wait_ge(sv.h, sv_h0 + 2 * fg + fc + 1)
                        shq.dma(s.dma_start(out=d_h[f * 128:(f + 1) * 128, :], in_=HROW[fc][:, :]))
            block.sync(_)

            def _(t):
                wait(t, sv)
                stm.inc(t.transpose(P[7].ap()[0:1, 0:128], t_maxacc[:], t_eye32[:]))
            block.tensor(_)

            def _(v):
                wait(v, stm)
                sv.inc(v.reduce_max(t_small.ap()[0:1, 12:13], P[7].ap()[0:1, 0:128], axis=AX.X))
            block.vector(_)

            def _(s):
                wait(s, sv)
                sq.dma(s.dma_start(out=cc2_in[0:1, 0:1], in_=t_small.ap()[0:1, 12:13]))
                sq.dma(s.dma_start(out=cc2_in[0:1, 1:8], in_=A_C32[0:1, 1608:1615]))
                sq.dma(s.dma_start(out=cc2_in[0:1, 8:264].rearrange("one (p c) -> (one p) c", p=128),
                                   in_=t_sca[:, 0:2]))
            block.sync(_)
            pay2_done = sq.n

            def _(g):
                g.wait_ge(sq.h, pay2_done)
                wait(g, gq)
                scc.inc(g.collective_compute("AllGather", ALU.bypass, replica_groups=[list(range(NCORES))],
                                             ins=[cc2_in.opt()], outs=[cc2_out.opt()]))
                g.wait_ge(scc.h, scc.n)
                gq.dma(g.dma_start(out=t_small.ap()[0:1, 40:48], in_=cc2_out[:, 0:1], allow_slow_non_contiguous=True))
                gq.dma(g.dma_start(out=t_small.ap()[0:1, 40:48], in_=cc2_out[:, 0:1], allow_slow_non_contiguous=True))
            block.gpsimd(_)
            ag2_done = gq.n

            def _(v):
                v.wait_ge(gq.h, ag2_done)
                sv.inc(v.reduce_max(t_small.ap()[0:1, 13:14], t_small.ap()[0:1, 40:48], axis=AX.X))
                v.wait_ge(sv.h, sv.n)
                sv.inc(v.tensor_scalar(t_small.ap()[0:1, 14:15], t_small.ap()[0:1, 13:14], 1e-8, None, ALU.max))
                v.wait_ge(sv.h, sv.n)
                sv.inc(v.tensor_scalar(t_small.ap()[0:1, 15:16], t_small.ap()[0:1, 14:15], 1.0 / 255.0, None, ALU.mult))
                v.wait_ge(sv.h, sv.n)
                sv.inc(v.reciprocal(t_small.ap()[0:1, 48:49], t_small.ap()[0:1, 15:16]))
            block.vector(_)

            def _(s):
                wait(s, sv)
                sq.dma(s.dma_start(out=d_row[3:4, 112:120], in_=t_small.ap()[0:1, 12:20]))
                sq.dma(s.dma_start(out=d_row[3:4, 120:121], in_=t_small.ap()[0:1, 48:49]))
                s.wait_ge(sq.h, sq.n)
                sq.dma(s.dma_start(out=t_sca[:, 2:3], in_=d_row[3:4, 120:121].partition_broadcast(128)))
                sq.dma(s.dma_start(out=t_sca[:, 3:4], in_=d_row[3:4, 115:116].partition_broadcast(128)))
            block.sync(_)
            sca2_done = sq.n

        if stage >= 5:
            SOFTT = A_C16[:, 0:8192]
            VVS = A_C16[:, 8192:12288]
            SOFTP = A_C16[:, 12288:12736]
            SOFTS = A_C16[:, 17856:18304]
            VWB = A_C16[0:PD, 12736:13760]
            GWS = [A_C16[:, 13760:15808], A_C16[:, 15808:17856]]
            RETR = A_B16
            RVALS = A_D32[0:1, 0:NCAND]
            RWORK = A_D32[0:1, 448:896]
            RG8b = A_D32[0:1, 896:904]
            RG56 = A_D32[0:1, 904:960]
            RMSK = A_D32[0:1, 960:1408]
            MREP = A_D32[:, 2048:2496]
            LM = A_D32[:, 2496:2944]
            PCOL = A_D32[:, 2944:2960]
            SIG = A_D32[:, 4096:4608]
            CGR = A_D32[:, 4608:5120]
            SUT = [A_D32[:, 5120:5632], A_D32[:, 6144:6656]]
            SFT = A_D32[:, 5632:6144]

            def _(s):
                wait(s, sv, ss, stm, sg, shq, sqw)
                sq.dma(s.dma_start(out=RVALS, in_=d_row[2:3, 0:NCAND]))
            block.sync(_)

            def _(g):
                wait(g, gq, sq)
                gq.dma(g.dma_start(out=VWB, in_=vw[:, :]))
            block.gpsimd(_)

            RSCR = A_D32[0:1, 1440:1888]

            def _(v):
                wait(v, sq, gq)
                v.tensor_copy(RWORK, RVALS)
                cur, other = RWORK, RSCR
                for r in range(7):
                    sv.inc(v.max(RG8b, cur))
                    v.wait_ge(sv.h, sv.n)
                    v.tensor_copy(RG56[:, r * 8:(r + 1) * 8], RG8b)
                    v.match_replace(other, RG8b, cur, -1.0)
                    cur, other = other, cur
                sv.inc(v.tensor_copy(t_small.ap()[0:1, 49:50], RG56[:, 49:50]))
                v.wait_ge(sv.h, sv.n)
                v.tensor_scalar(RSCR, RVALS, 0.5, None, ALU.is_gt)
                v.tensor_scalar(RMSK, RVALS, t_small.ap()[0:1, 49:50], None, ALU.is_ge)
                v.tensor_tensor(RMSK, RMSK, RSCR, ALU.mult)
                sv.inc(v.tensor_scalar(RMSK, RMSK, 1e9, -1e9, ALU.mult, ALU.add))
            block.vector(_)

            def _(s):
                wait(s, sv)
                sq.dma(s.dma_start(out=d_row[3:4, 128:128 + NCAND], in_=RMSK))
                s.wait_ge(sq.h, sq.n)
                sq.dma(s.dma_start(out=MREP[:, :], in_=d_row[3:4, 128:128 + NCAND].partition_broadcast(128)))
            block.sync(_)
            mrep_done = sq.n

            st_kv0 = stm.n
            sv_kv0 = sv.n

            def _(t, sv_kv0=sv_kv0):
                wait(t, sv, gq)
                stm.inc(t.matmul(P[0].ap()[0:PD, 0:NCAND], t_kww.ap(), t_poolall.ap(), start=True, stop=True))
                for k in range(8):
                    sc4, nb2 = k // 2, k % 2
                    if k >= 2:
                        t.wait_ge(sv.h, sv_kv0 + 1 + (k - 1))
                    stm.inc(t.matmul(P[1 + k % 2].ap()[0:112, 0:512],
                                     t_poolall.ap()[:, sc4 * 112:(sc4 + 1) * 112],
                                     VWB[:, nb2 * 512:(nb2 + 1) * 512], start=True, stop=True))
            block.tensor(_)

            def _(v, st_kv0=st_kv0):
                v.wait_ge(stm.h, st_kv0 + 1)
                sv.inc(v.tensor_copy(t_kkT.ap(), P[0].ap()[0:PD, 0:NCAND]))
                for k in range(8):
                    sc4, nb2 = k // 2, k % 2
                    v.wait_ge(stm.h, st_kv0 + k + 2)
                    sv.inc(v.tensor_copy(VVS[0:112, sc4 * 1024 + nb2 * 512:sc4 * 1024 + (nb2 + 1) * 512],
                                         P[1 + k % 2].ap()[0:112, 0:512]))
            block.vector(_)

            st_sm0 = stm.n
            sv_sm0 = sv.n
            ss_sm0 = ss.n

            def _(t):
                wait(t, sv)
                t.wait_ge(sq.h, mrep_done)
                for tc in range(16):
                    if tc >= 1:
                        t.wait_ge(sv.h, sv_sm0 + 7 * (tc - 1) + 1)
                    stm.inc(t.matmul(P[6].ap()[:, 0:NCAND], t_qT.ap()[:, tc * 128:(tc + 1) * 128], t_kkT.ap(),
                                     start=True, stop=True))
                    t.wait_ge(sv.h, sv_sm0 + 7 * tc + 6)
                    last = None
                    for sc4 in range(4):
                        last = t.matmul(PB[7][0:112, sc4 * 128:(sc4 + 1) * 128],
                                        SOFTS[:, sc4 * 112:(sc4 + 1) * 112], t_eye16[:],
                                        is_transpose=True, start=True, stop=True)
                    stm.inc(last)
            block.tensor(_)

            def _(v):
                for tc in range(16):
                    v.wait_ge(stm.h, st_sm0 + 2 * tc + 1)
                    v.tensor_tensor(LM, P[6].ap()[:, 0:NCAND], MREP[:, :], ALU.add)
                    sv.inc(v.reduce_max(PCOL[:, 0:1], LM, axis=AX.X, negate=True))
                    v.wait_ge(sv.h, sv.n)
                    sv.inc(v.tensor_scalar(PCOL[:, 1:2], PCOL[:, 0:1], 0.25, None, ALU.mult))
                    v.wait_ge(ss.h, ss_sm0 + tc + 1)
                    sv.inc(v.reduce_sum(PCOL[:, 2:3], SOFTP, axis=AX.X))
                    v.wait_ge(sv.h, sv.n)
                    sv.inc(v.tensor_scalar(PCOL[:, 4:5], PCOL[:, 2:3], 1.0, 1e-30, ALU.mult, ALU.add))
                    v.wait_ge(sv.h, sv.n)
                    sv.inc(v.reciprocal(PCOL[:, 5:6], PCOL[:, 4:5]))
                    v.wait_ge(sv.h, sv.n)
                    sv.inc(v.tensor_scalar(SOFTS, SOFTP, PCOL[:, 5:6], None, ALU.mult))
                    v.wait_ge(stm.h, st_sm0 + 2 * tc + 2)
                    last = None
                    for sc4 in range(4):
                        last = v.tensor_copy(SOFTT[0:112, sc4 * 2048 + tc * 128:sc4 * 2048 + (tc + 1) * 128],
                                             PB[7][0:112, sc4 * 128:(sc4 + 1) * 128])
                    sv.inc(last)
            block.vector(_)

            def _(sc):
                for tc in range(16):
                    sc.wait_ge(sv.h, sv_sm0 + 7 * tc + 2)
                    ss.inc(sc.activation(out=SOFTP, in_=LM, func=AF.Exp, bias=PCOL[:, 1:2], scale=0.25))
            block.scalar(_)

            st_rt0 = stm.n
            sv_rt0 = sv.n

            def _(t):
                wait(t, sv, ss)
                k = 0
                for dc in range(8):
                    for tbk in range(4):
                        bank = k % 4
                        if k >= 4:
                            t.wait_ge(sv.h, sv_rt0 + (k - 4) + 1)
                        last = None
                        for sc4 in range(4):
                            last = t.matmul(P[bank].ap()[:, 0:512],
                                            VVS[0:112, sc4 * 1024 + dc * 128:sc4 * 1024 + dc * 128 + 128],
                                            SOFTT[0:112, sc4 * 2048 + tbk * 512:sc4 * 2048 + (tbk + 1) * 512],
                                            start=(sc4 == 0), stop=(sc4 == 3))
                        stm.inc(last)
                        k += 1
            block.tensor(_)

            def _(v):
                for k in range(32):
                    dc, tbk = k // 4, k % 4
                    bank = k % 4
                    v.wait_ge(stm.h, st_rt0 + k + 1)
                    sv.inc(v.tensor_copy(RETR[:, dc * T + tbk * 512:dc * T + (tbk + 1) * 512], P[bank].ap()[:, 0:512]))
            block.vector(_)
            retr_done = sv.n

            st_g0 = stm.n
            gq_g0 = gq.n
            sv_g0 = sv.n
            ssv = Sem(nc, ctx, "ssv")

            def _(g, gq_g0=gq_g0):
                wait(g, gq, sv)
                for dco in range(8):
                    if dco >= 2:
                        g.wait_ge(stm.h, st_g0 + 4 * (dco - 1))
                    gq.dma(g.dma_start(out=GWS[dco % 2].rearrange("p (c f) -> p c f", f=128),
                                       in_=gatew[:, dco * 128:(dco + 1) * 128].rearrange("(c p) f -> p c f", p=128)))
            block.gpsimd(_)

            sq_su_g = sq.n

            def _(s):
                wait(s, sv)
                for k in range(33):
                    if k < 32:
                        dco, tbk = k // 4, k % 4
                        if k >= 2:
                            s.wait_ge(sv.h, sv_g0 + (k - 1))
                        sq.dma(s.dma_start(out=SUT[k % 2][:, :], in_=d_su[dco * 128:(dco + 1) * 128, tbk * 512:(tbk + 1) * 512]))
                    if k == 32:
                        sq.dma(s.dma_start(out=SUT[31 % 2][:, :], in_=d_su[7 * 128:8 * 128, 3 * 512:4 * 512]))
                    if k >= 1:
                        dcp, tbp = (k - 1) // 4, (k - 1) % 4
                        s.wait_ge(sv.h, sv_g0 + k)
                        ssv.dma(s.dma_start(out=d_s[dcp * 128:(dcp + 1) * 128, tbp * 512:(tbp + 1) * 512], in_=SFT[:, :]))
            block.sync(_)

            ss_g0 = ss.n

            def _(t, gq_g0=gq_g0):
                wait(t, sv)
                for dco in range(8):
                    t.wait_ge(gq.h, gq_g0 + 16 * (dco + 1))
                    for tbk in range(4):
                        bank = 4 + tbk % 2
                        if dco + tbk > 0:
                            t.wait_ge(ss.h, ss_g0 + 4 * dco + tbk - 1)
                        last = None
                        for gi in range(16):
                            rhs = (A_A[:, gi * T + tbk * 512:gi * T + (tbk + 1) * 512] if gi < 8 else
                                   RETR[:, (gi - 8) * T + tbk * 512:(gi - 8) * T + (tbk + 1) * 512])
                            last = t.matmul(P[bank].ap()[:, 0:512], GWS[dco % 2][:, gi * 128:(gi + 1) * 128],
                                            rhs, start=(gi == 0), stop=(gi == 15))
                        stm.inc(last)
            block.tensor(_)

            def _(sc):
                for k in range(32):
                    dco, tbk = k // 4, k % 4
                    bank = 4 + tbk % 2
                    sc.wait_ge(stm.h, st_g0 + k + 1)
                    if k >= 1:
                        sc.wait_ge(sv.h, sv_g0 + k)
                    ss.inc(sc.activation(out=SIG[:, :], in_=P[bank].ap()[:, 0:512], func=AF.Sigmoid, bias=0.0, scale=1.0))
            block.scalar(_)

            def _(v):
                for k in range(32):
                    dco, tbk = k // 4, k % 4
                    v.wait_ge(ss.h, ss_g0 + k + 1)
                    v.wait_ge(sq.h, sq_su_g + 16 * (k + 2))
                    if k >= 1:
                        v.wait_ge(ssv.h, 16 * k)
                    v.scalar_tensor_tensor(CGR[:, :], SIG[:, :], t_sca.ap()[:, 4:5],
                                           RETR[:, dco * T + tbk * 512:dco * T + (tbk + 1) * 512], ALU.mult, ALU.mult)
                    sv.inc(v.tensor_tensor(SFT[:, :], CGR[:, :], SUT[k % 2][:, :], ALU.add))
            block.vector(_)



        if stage >= 6:
            XQ2 = A_C16
            HS = [A_A32[:, 0:1024], A_A32[:, 1024:2048]]
            HI32 = A_AI[:, 2048:3072]
            WQ2S = [A_D16[:, 0:512], A_D16[:, 512:1024]]
            GRow = A_D32[:, 4096:4608]
            SRow = [A_D32[:, 4608:5120], A_D32[:, 6144:6656]]
            XRow = [A_D32[:, 5120:5632], A_D32[:, 6656:7168]]
            ORow = A_D32[:, 5632:6144]

            def _(s):
                wait(s, sv, ss, stm, sg, ssv)
                sq.dma(s.dma_start(out=t_bcol[:, 0:8], in_=b2v.rearrange("(c p) -> p c", p=128), allow_slow_non_contiguous=True))
                sq.dma(s.dma_start(out=t_bcol[:, 0:8], in_=b2v.rearrange("(c p) -> p c", p=128), allow_slow_non_contiguous=True))
            block.sync(_)

            def _(v):
                wait(v, sq)
                sv.inc(v.tensor_tensor(t_scl2[:], t_sw2[:], t_sca.ap()[:, 3:4].to_broadcast([128, 8]), ALU.mult))
                v.wait_ge(sv.h, sv.n)
                sv.inc(v.reciprocal(t_bi32.ap().bitcast(F32)[:, 0:8], t_scl2[:]))
                v.wait_ge(sv.h, sv.n)
                sv.inc(v.tensor_tensor(t_bi32[:, 0:8], t_bcol[:, 0:8], t_bi32.ap().bitcast(F32)[:, 0:8], ALU.mult))
                v.wait_ge(sv.h, sv.n)
                sv.inc(v.tensor_tensor(t_b2q[:], t_bi32[:, 0:8], t_scl2[:], ALU.mult))
            block.vector(_)

            for tbg in range(2):
                sq_h0 = sq.n
                sv_q20 = sv.n

                def _(s, tbg=tbg, sv_q20=sv_q20):
                    wait(s, sv)
                    for f in range(32):
                        if f >= 2:
                            s.wait_ge(sv.h, sv_q20 + (f - 1))
                        sq.dma(s.dma_start(out=HS[f % 2][:, :], in_=d_h[f * 128:(f + 1) * 128, tbg * 1024:(tbg + 1) * 1024]))
                    sq.dma(s.dma_start(out=HS[31 % 2][:, :], in_=d_h[31 * 128:32 * 128, tbg * 1024:(tbg + 1) * 1024]))
                block.sync(_)

                def _(v, sq_h0=sq_h0):
                    for f in range(32):
                        v.wait_ge(sq.h, sq_h0 + 16 * (f + 2))
                        v.tensor_scalar(HI32, HS[f % 2][:, :], t_sca.ap()[:, 2:3], None, ALU.mult)
                        sv.inc(v.tensor_scalar(XQ2[:, f * 1024:(f + 1) * 1024], HI32, 255.0, 0.0, ALU.min, ALU.max))
                block.vector(_)
                xq2_done = sv.n

                for ddg in range(2):
                    st_m2 = stm.n
                    gq_w2 = gq.n
                    sv_o0 = sv.n
                    sfin_b = sfin.n

                    def _(g, ddg=ddg, gq_w2=gq_w2, st_m2=st_m2):
                        wait(g, gq, sv)
                        for f in range(32):
                            if f >= 2:
                                g.wait_ge(stm.h, st_m2 + 8 * (f - 1))
                            gq.dma(g.dma_start(out=WQ2S[f % 2][:, :],
                                               in_=d_wq2[f * 128:(f + 1) * 128, ddg * 512:(ddg + 1) * 512]))
                    block.gpsimd(_)

                    def _(t, ddg=ddg, gq_w2=gq_w2, st_m2=st_m2):
                        wait(t, sv)
                        for f in range(32):
                            t.wait_ge(gq.h, gq_w2 + 16 * (f + 1))
                            last = None
                            for dd in range(4):
                                for tb2 in range(2):
                                    bank = dd * 2 + tb2
                                    last = t.matmul(P[bank].ap()[:, :],
                                                    WQ2S[f % 2][:, dd * 128:(dd + 1) * 128],
                                                    XQ2[:, f * 1024 + tb2 * 512:f * 1024 + (tb2 + 1) * 512],
                                                    start=(f == 0), stop=(f == 31))
                                    stm.inc(last)
                    block.tensor(_)
                    mm2_done = stm.n

                    def _(sc, ddg=ddg, mm2_done=mm2_done):
                        sc.wait_ge(stm.h, mm2_done)
                        for dd in range(4):
                            for tb2 in range(2):
                                bank = dd * 2 + tb2
                                d_o = ddg * 4 + dd
                                ss.inc(sc.activation(out=GRow[:, :], in_=P[bank].ap()[:, :], func=AF.Silu,
                                                     bias=t_b2q.ap()[:, d_o:d_o + 1], scale=t_scl2.ap()[:, d_o:d_o + 1]))
                                sc.wait_ge(sv.h, sv_o0 + (dd * 2 + tb2) * 1 + 1)
                    block.scalar(_)

                    sq_fb = sq.n
                    sv_o1 = sv_o0

                    def _(s, tbg=tbg, ddg=ddg, sv_o0=sv_o0, sv_o1=sv_o1):
                        wait(s, ssv, shq)
                        for kk2 in range(9):
                            if kk2 < 8:
                                dd, tb2 = kk2 // 2, kk2 % 2
                                d_o = ddg * 4 + dd
                                tglob = tbg * 1024 + tb2 * 512
                                if kk2 >= 2:
                                    s.wait_ge(sv.h, sv_o0 + kk2 - 1)
                                sq.dma(s.dma_start(out=SRow[kk2 % 2][:, :], in_=d_s[d_o * 128:(d_o + 1) * 128, tglob:tglob + 512]))
                                sq.dma(s.dma_start(out=XRow[kk2 % 2][:, :], in_=x_f[d_o * 128:(d_o + 1) * 128, tglob:tglob + 512]))
                            if kk2 == 8:
                                d_o = ddg * 4 + 3
                                tglob = tbg * 1024 + 512
                                sq.dma(s.dma_start(out=SRow[7 % 2][:, :], in_=d_s[d_o * 128:(d_o + 1) * 128, tglob:tglob + 512]))
                                sq.dma(s.dma_start(out=XRow[7 % 2][:, :], in_=x_f[d_o * 128:(d_o + 1) * 128, tglob:tglob + 512]))
                            if kk2 >= 1:
                                dd, tb2 = (kk2 - 1) // 2, (kk2 - 1) % 2
                                d_o = ddg * 4 + dd
                                tglob = tbg * 1024 + tb2 * 512
                                s.wait_ge(sv.h, sv_o1 + kk2)
                                sfin.dma(s.dma_start(out=out_f[d_o * 128:(d_o + 1) * 128, tglob:tglob + 512], in_=ORow[:, :]))
                    block.sync(_)

                    ss_o0 = ss.n - 8

                    def _(v, tbg=tbg, ddg=ddg, ss_o0=ss_o0, sq_fb=sq_fb, sv_o0=sv_o0):
                        for kk2 in range(8):
                            v.wait_ge(ss.h, ss_o0 + kk2 + 1)
                            v.wait_ge(sq.h, sq_fb + 32 * (kk2 + 2))
                            if kk2 >= 1:
                                v.wait_ge(sfin.h, sfin_b + 16 * kk2)
                            v.tensor_tensor(ORow[:, :], SRow[kk2 % 2][:, :], GRow[:, :], ALU.mult)
                            sv.inc(v.tensor_tensor(ORow[:, :], ORow[:, :], XRow[kk2 % 2][:, :], ALU.add))
                    block.vector(_)

        def _(v):
            wait(v, sq, gq, ss, stm, sg, scc, szs[0], szs[1], sxs[0], sxs[1], ssu)
            sv.inc(v.memset(t_sca[:, 7:8], 0.0))
        block.vector(_)

        dbg_dram = {
            "xn": (d_xn, [D, T], F32), "z": (d_z, [DI, T], BF16), "xs": (d_xs, [DI, T], BF16),
            "su": (d_su, [D, T], F32), "h": (d_h, [FFD, T], F32), "s": (d_s, [D, T], F32),
            "wq1": (d_wq1, [D, FFD], BF16), "wq2": (d_wq2, [FFD, D], BF16),
        }
        sbuf_dbg = {
            "dt": lambda: t_hrows[0:32, :], "cs": lambda: t_hrows[64:96, :],
            "bc": lambda: t_bc[:, :], "tmv": lambda: t_tm[:, :],
            "scores": lambda: t_scores[:, :], "summtm": lambda: t_summtm[:, :],
            "qTv": lambda: t_qT[:, :], "y2": lambda: A_A[:, :], "dall": lambda: t_dall[:, :],
            "poolall": lambda: t_poolall[:, :], "kkT": lambda: t_kkT[:, :],
            "mrep": lambda: A_D32[:, 2048:2496], "softT": lambda: A_C16[0:112, 0:8192],
            "vvs": lambda: A_C16[0:112, 8192:12288],
            "small": lambda: t_small[:, :], "retr": lambda: AR_B.ap()[:, :],
            "xq2": lambda: A_C16[:, :], "scl2": lambda: t_scl2[:, :], "b2q": lambda: t_b2q[:, :],
            "summtm": lambda: t_summtm[:, :],
        }

        def _(g):
            wait(g, sv)
            for name, shape in dbg:
                dst = dbg_out[name]
                if name in dbg_dram:
                    srct, sh, dtp = dbg_dram[name]
                    for r0 in range(0, sh[0], 128):
                        rw = min(128, sh[0] - r0)
                        tile = A_D32[:, 0:sh[1]] if dtp == F32 else A_D16[:, 0:sh[1]]
                        gq.dma(g.dma_start(out=tile[0:rw, :], in_=srct[r0:r0 + rw, :]))
                        g.wait_ge(gq.h, gq.n)
                        gq.dma(g.dma_start(out=dst[r0:r0 + rw, :], in_=tile[0:rw, :]))
                        g.wait_ge(gq.h, gq.n)
                else:
                    srcap = sbuf_dbg[name]()
                    gq.dma(g.dma_start(out=dst[0:shape[0], 0:shape[1]], in_=srcap))
                    g.wait_ge(gq.h, gq.n)
        block.gpsimd(_)

        if stage >= 6:
            def _(s):
                wait(s, sq, gq, sv, ss, stm, sg, scc, ssv, shq, sqw, sfin)
            block.sync(_)
        else:
            def _(v):
                wait(v, sq)
                sv.inc(v.memset(A_D32[:, 0:16], 1.0))
            block.vector(_)

            def _(s):
                wait(s, sv)
                sfin.dma(s.dma_start(out=out_f[0:128, 0:16], in_=A_D32[:, 0:16]))
                s.wait_ge(sfin.h, sfin.n)
            block.sync(_)

    return nc


def make_inputs(f, core):
    x = np.ascontiguousarray(np.asarray(f["L0_attn_x"][core]).T)
    eye = np.eye(128, dtype=np.float32)
    tri = (np.arange(128)[:, None] <= np.arange(128)[None, :]).astype(np.float32)
    return {
        "x_f": x,
        "normw": np.asarray(f["L0_attn_norm_w"]),
        "inprojw": np.asarray(f["L0_attn_in_proj_w"]),
        "convw": np.asarray(f["L0_attn_conv_w"]),
        "convb": np.asarray(f["L0_attn_conv_b"]),
        "dtbias": np.asarray(f["L0_attn_dt_bias"]),
        "alog": np.asarray(f["L0_attn_A_log"]),
        "dvec": np.asarray(f["L0_attn_D"]),
        "ssmw": np.asarray(f["L0_attn_ssm_norm_w"]),
        "outpw": np.asarray(f["L0_attn_out_proj_w"]),
        "sc1w": np.asarray(f["L0_attn_scorer_w1"]),
        "sc2w": np.asarray(f["L0_attn_scorer_w2"]),
        "summw": np.asarray(f["L0_attn_summ_w"]),
        "qw": np.asarray(f["L0_attn_q_w"]),
        "kw": np.asarray(f["L0_attn_k_w"]),
        "vw": np.asarray(f["L0_attn_v_w"]),
        "gatew": np.asarray(f["L0_attn_gate_w"]),
        "w1t": np.ascontiguousarray(np.asarray(f["L0_ffn_w1"]).T),
        "b1v": np.asarray(f["L0_ffn_b1"]),
        "w2t": np.ascontiguousarray(np.asarray(f["L0_ffn_w2"]).T),
        "b2v": np.asarray(f["L0_ffn_b2"]),
        "c_eye": eye,
        "c_tri": tri,
        "c_ones": np.ones((128, 1), np.float32),
        "c_onesr": np.ones((1, 128), np.float32),
    }


_NC_CACHE = {}


def kernel(**inputs):
    from concourse.bass_utils import run_bass_kernel_spmd
    if "nc" not in _NC_CACHE:
        _NC_CACHE["nc"] = build_nc(dbg=(), stage=99)
    ncobj = _NC_CACHE["nc"]
    in_maps = [make_inputs(inputs, c) for c in range(NCORES)]
    res = run_bass_kernel_spmd(ncobj, in_maps, core_ids=list(range(NCORES)))
    out = np.stack([np.ascontiguousarray(res.results[c]["out_f"].T) for c in range(NCORES)], axis=0)
    return out
